# version 03 (final)
# baseline (speedup 1.0000x reference)
import numpy as np
import contextlib
from collections import Counter

import concourse.bass as bass
import concourse.mybir as mybir
from concourse.bass_utils import run_bass_kernel_spmd

F32 = mybir.dt.float32
BF16 = mybir.dt.bfloat16
AF = mybir.ActivationFunctionType
ALU = mybir.AluOpType
AX = mybir.AxisListType

B, S, E, H, DH = 4, 4096, 2048, 16, 128
MAGIC = float(np.float32(1.5 * 2 ** 23))
NEGBIG = -1.0e9
SCALE = float(1.0 / np.sqrt(np.float32(DH)))
EPS = 1e-5
QMAX = 127.0
TINY = 1e-30
SHUF_SWAP = [x for p2 in range(16) for x in (2 * p2 + 1, 2 * p2)]


class Emitter:
    def __init__(self):
        self.plan = {}
        self.cnt = Counter()
        self.mode = "plan"
        self.api = None
        self.sems = None
        self.last_wait = {}

    def set_plan(self):
        self.mode = "plan"
        self.api = None

    def set_engine(self, eng, api, sems):
        self.mode = eng
        self.api = api
        self.sems = sems

    def alias(self, new_key, old_key):
        if self.mode == "plan":
            self.plan[new_key] = self.plan[old_key]

    def op(self, eng, emit, waits=(), inc=None, key=None, n=1, pre=False):
        if self.mode == "plan":
            if inc is not None:
                self.cnt[inc] += n
                if key is not None:
                    assert key not in self.plan, key
                    self.plan[key] = (inc, self.cnt[inc])
            return
        if self.mode != eng:
            return
        wl = []
        if pre:
            assert inc is not None and key is not None
            sem, val = self.plan[key]
            if val - n > 0:
                wl.append((sem, val - n))
        for w in waits:
            if w is None:
                continue
            if isinstance(w, str):
                wl.append(self.plan[w])
            else:
                wl.append(w)
        for sem, val in wl:
            if val <= 0:
                continue
            lw = self.last_wait.get((eng, sem), 0)
            if val > lw:
                self.api.wait_ge(self.sems[sem], val)
                self.last_wait[(eng, sem)] = val
        instr = emit(self.api)
        if inc is not None:
            assert instr is not None, key
            instr.then_inc(self.sems[inc], n)


def build(dump=None, nu=64):
    nc = bass.Bass("TRN2", target_bir_lowering=False, debug=False)
    di = lambda name, shape: nc.dram_tensor(name, shape, F32, kind="ExternalInput").ap()

    xT_st = di("xT_st", [8, 128, 16, 512])
    xTq_st = di("xTq_st", [4, 128, 16, 512])
    xtm_st = di("xtm_st", [16, 128, 2048])
    wq_st = di("wq_st", [128, 16, 16, 128])
    wk_st = di("wk_st", [128, 16, 16, 128])
    wv_st = di("wv_st", [128, 16, 4, 512])
    wo_st = di("wo_st", [128, 16, 4, 512])
    fw_st = di("fw_st", [2048, 2048])
    fwt_st = di("fwt_st", [16, 128, 2048])
    bq_r = di("bq_r", [1, 2048])
    bk_r = di("bk_r", [1, 2048])
    bv_r = di("bv_r", [1, 2048])
    fb_r = di("fb_r", [1, 2048])
    bo_r = di("bo_r", [1, 2048])
    cos_st = di("cos_st", [128, 4096])
    sin_st = di("sin_st", [128, 4096])
    cosq_st = di("cosq_st", [128, 2048])
    sinq_st = di("sinq_st", [128, 2048])
    mask_st = di("mask_st", [128, 8, 512])
    id_st = di("id_st", [128, 128])

    y_out = nc.dram_tensor("y_out", [16, 128, 2048], F32, kind="ExternalOutput").ap()

    def scr_t(name, shape, dt):
        ext = (dump == "all"
               or (dump == "qkv" and name in ("q_s", "k_s", "v_s"))
               or (dump == "attn" and name in ("a_s",)))
        kind = "ExternalOutput" if ext else "Internal"
        return nc.dram_tensor(name, shape, dt, kind=kind).ap()

    q_s = scr_t("q_s", [16, 128, 2048], BF16)
    k_s = scr_t("k_s", [16, 128, 4096], BF16)
    v_s = scr_t("v_s", [32, 128, 2048], BF16)
    a_s = scr_t("a_s", [16, 16, 128, 128], BF16)
    ln1_s = scr_t("ln1_s", [16, 128, 2048], F32)
    xq_s = scr_t("xq_s", [16, 128, 2048], BF16)
    dbg_pn = scr_t("dbg_pn", [16, 128, 4096], BF16) if dump == "all" else None
    dbg_z = scr_t("dbg_z", [128, 64], F32) if dump == "all" else None
    dbg_r1 = scr_t("dbg_r1", [16, 128, 2048], F32) if dump == "all" else None
    dbg_at = scr_t("dbg_at", [16, 128, 2048], BF16) if dump == "all" else None
    dbg_st = scr_t("dbg_st", [16, 128, 10], F32) if dump == "all" else None
    dbg_kq = scr_t("dbg_kq", [128, 16 * 2048], BF16) if dump == "all" else None
    dbg_xt = scr_t("dbg_xt", [16, 128, 2048], BF16) if dump == "all" else None
    dbg_r2 = scr_t("dbg_r2", [16, 128, 2048], F32) if dump == "all" else None
    dbg_s4 = scr_t("dbg_s4", [16, 128, 10], F32) if dump == "all" else None
    dbg_fb = scr_t("dbg_fb", [3, 128, 2048], F32) if dump == "all" else None
    dbg_yr = scr_t("dbg_yr", [16, 128, 2048], F32) if dump == "all" else None
    dbg_ys = scr_t("dbg_ys", [16, 128, 2048], F32) if dump == "all" else None

    em = Emitter()
    es = contextlib.ExitStack()
    with es:
        sem_names = ["dX0", "dX1", "dW", "dS0", "dS1",
                     "dK0", "dK1", "dK2", "dK3", "dK4", "dK5",
                     "dT0", "dT1", "dT2", "dT3", "dO0", "dO1",
                     "cP", "cV", "cA"]
        sems = {s: es.enter_context(nc.semaphore(s)) for s in sem_names}
        block = es.enter_context(nc.Block())

        g = {}
        for nm, shape, dt in [
            ("id_f", [128, 128], F32),
            ("ones_b", [1, 512], BF16), ("ones_f", [1, 128], F32),
            ("sx_all", [128, 16], F32),
        ]:
            g[nm] = es.enter_context(nc.sbuf_tensor(nm, shape, dt))
        p12 = es.enter_context(contextlib.ExitStack())
        for nm, shape, dt in [
            ("id_b", [128, 128], BF16), ("mask_b", [128, 8 * 512], BF16),
            ("bq_b", [1, 2048], BF16), ("bk_b", [1, 2048], BF16), ("bv_b", [1, 2048], BF16),
        ]:
            g[nm] = p12.enter_context(nc.sbuf_tensor(nm, shape, dt))

        def run_phase(phase_fn):
            em.set_plan()
            phase_fn(em)
            for eng, attr in (("pe", "tensor"), ("dve", "vector"), ("act", "scalar"),
                              ("sync", "sync"), ("gp", "gpsimd")):
                def section(api, eng=eng):
                    em.set_engine(eng, api, sems)
                    phase_fn(em)
                getattr(block, attr)(section)

        with contextlib.ExitStack() as p1:
            wfull = p1.enter_context(nc.sbuf_tensor("wfull", [128, 32768], BF16))
            xst = [p1.enter_context(nc.sbuf_tensor(f"xst{i}", [128, 8192], BF16)) for i in range(2)]
            cos_b = p1.enter_context(nc.sbuf_tensor("cos_b", [128, 4096], BF16))
            sin_b = p1.enter_context(nc.sbuf_tensor("sin_b", [128, 4096], BF16))
            cosq_b = p1.enter_context(nc.sbuf_tensor("cosq_b", [128, 2048], BF16))
            sinq_b = p1.enter_context(nc.sbuf_tensor("sinq_b", [128, 2048], BF16))
            sh_b = [p1.enter_context(nc.sbuf_tensor(f"sh{i}", [128, 512], F32)) for i in range(2)]
            u_f = [p1.enter_context(nc.sbuf_tensor(f"uf{i}", [128, 512], F32)) for i in range(2)]
            t_b = [p1.enter_context(nc.sbuf_tensor(f"tb{i}", [128, 512], BF16)) for i in range(2)]
            qv_b = [p1.enter_context(nc.sbuf_tensor(f"qv{i}", [128, 512], BF16)) for i in range(2)]
            vev = [p1.enter_context(nc.sbuf_tensor(f"vev{i}", [128, 512], BF16)) for i in range(2)]
            pp = [p1.enter_context(nc.psum_tensor(f"pp{i}", [128, 512], F32)) for i in range(4)]

            def phase1(K):
                setup = [
                    ("id", g["id_b"], id_st, None), ("idf", g["id_f"], id_st, None),
                    ("mask", g["mask_b"], mask_st, 8),
                    ("bqr", g["bq_b"], bq_r, None), ("bkr", g["bk_b"], bk_r, None),
                    ("bvr", g["bv_b"], bv_r, None),
                    ("cos", cos_b, cos_st, None), ("sin", sin_b, sin_st, None),
                    ("cosq", cosq_b, cosq_st, None), ("sinq", sinq_b, sinq_st, None),
                ]
                for nm, dst, src, grp in setup:
                    if grp:
                        K.op("gp", lambda a, d=dst, s=src, grp=grp: a.dma_start(
                            out=d[:].rearrange("p (g f) -> p g f", g=grp), in_=s[:]),
                            inc="dW", key=f"w_{nm}", n=16, pre=True)
                    else:
                        K.op("gp", lambda a, d=dst, s=src: a.dma_start(out=d[:], in_=s[:]),
                             inc="dW", key=f"w_{nm}", n=16, pre=True)
                K.op("dve", lambda a: a.memset(g["ones_b"][:], 1.0))
                K.op("dve", lambda a: a.memset(g["ones_f"][:], 1.0),
                     inc="cV", key="ones_ready")

                bank_free = {i: None for i in range(4)}
                xst_free = {0: None, 1: None}

                def proj_qk(sub, w_src, n_tg, xsrc, bias_row, cos_t, sin_t, out_s, prev_w_user):
                    wkey = f"w_{sub}w"
                    K.op("gp", lambda a: a.dma_start(
                        out=wfull[:].rearrange("p (c e f) -> p c e f", c=16, e=16),
                        in_=w_src[:]),
                        waits=[prev_w_user] if prev_w_user else [],
                        inc="dW", key=wkey, n=16, pre=True)
                    last = None
                    for tg in range(n_tg):
                        par = tg % 2
                        xkey = f"{sub}_x{tg}"
                        K.op("gp", lambda a, tg=tg, par=par: a.dma_start(
                            out=xst[par][:].rearrange("p (e f) -> p e f", e=16),
                            in_=xsrc[tg]),
                            waits=[xst_free[par]] if xst_free[par] else [],
                            inc=f"dX{par}", key=xkey, n=16, pre=True)
                        for c in range(16):
                            i = tg * 16 + c
                            bank = i % 4
                            bw = [xkey, wkey, bank_free[bank]]

                            def mm(a, c=c, par=par, bank=bank):
                                a.matmul(pp[bank][:], bias_row[0:1, c * 128:(c + 1) * 128],
                                         g["ones_b"][:], start=True, stop=False)
                                r = None
                                for e in range(16):
                                    r = a.matmul(
                                        pp[bank][:],
                                        wfull[:, (c * 16 + e) * 128:(c * 16 + e + 1) * 128],
                                        xst[par][:, e * 512:(e + 1) * 512],
                                        start=False, stop=(e == 15))
                                return r
                            K.op("pe", mm, waits=bw, inc="cP", key=f"{sub}_mm_{i}")
                            ep = i % 2
                            K.op("dve", lambda a, bank=bank, ep=ep: a.stream_shuffle(
                                sh_b[ep][:], pp[bank][:], SHUF_SWAP),
                                waits=[f"{sub}_mm_{i}", "w_cos",
                                       f"{sub}_spill_{i-2}" if i >= 2 else None])
                            K.op("dve", lambda a, bank=bank, ep=ep, tg=tg: a.tensor_tensor(
                                u_f[ep][:], pp[bank][:],
                                cos_t[:, tg * 512:(tg + 1) * 512], ALU.mult),
                                inc="cV", key=f"{sub}_ropeC_{i}")
                            bank_free[bank] = f"{sub}_ropeC_{i}"
                            K.op("dve", lambda a, ep=ep, tg=tg: a.tensor_tensor(
                                t_b[ep][:], sh_b[ep][:],
                                sin_t[:, tg * 512:(tg + 1) * 512], ALU.mult))
                            K.op("dve", lambda a, ep=ep: a.tensor_tensor(
                                qv_b[ep][:], u_f[ep][:], t_b[ep][:], ALU.add),
                                inc="cV", key=f"{sub}_ropeD_{i}")
                            K.op("sync", lambda a, c=c, tg=tg, ep=ep: a.dma_start(
                                out=out_s[c][:, tg * 512:(tg + 1) * 512], in_=qv_b[ep][:]),
                                waits=[f"{sub}_ropeD_{i}"],
                                inc=f"dS{ep}", key=f"{sub}_spill_{i}", n=16, pre=True)
                            last = f"{sub}_mm_{i}"
                        xst_free[par] = last
                    return last

                lk = proj_qk("q", wq_st, 4, xTq_st, g["bq_b"], cosq_b, sinq_b, q_s, None)
                lk = proj_qk("k", wk_st, 8, xT_st, g["bk_b"], cos_b, sin_b, k_s, lk)

                K.op("gp", lambda a: a.dma_start(
                    out=wfull[:].rearrange("p (e d f) -> p e d f", e=16, d=4),
                    in_=wv_st[:]), waits=[lk], inc="dW", key="w_vw", n=16, pre=True)
                for tg in range(8):
                    par = tg % 2
                    xkey = f"v_x{tg}"
                    K.op("gp", lambda a, tg=tg, par=par: a.dma_start(
                        out=xst[par][:].rearrange("p (e f) -> p e f", e=16),
                        in_=xT_st[tg]),
                        waits=[xst_free[par]] if xst_free[par] else [],
                        inc=f"dX{par}", key=xkey, n=16, pre=True)
                    for kt4 in range(4):
                        for dg in range(4):
                            vi = (tg * 4 + kt4) * 4 + dg
                            ktile = tg * 4 + kt4
                            bank = vi % 4
                            ep = vi % 2

                            def mmv(a, par=par, kt4=kt4, dg=dg, bank=bank):
                                a.matmul(pp[bank][:], g["ones_b"][0:1, 0:128],
                                         g["bv_b"][0:1, dg * 512:(dg + 1) * 512],
                                         start=True, stop=False)
                                r = None
                                for e in range(16):
                                    r = a.matmul(
                                        pp[bank][:],
                                        xst[par][:, e * 512 + kt4 * 128: e * 512 + (kt4 + 1) * 128],
                                        wfull[:, (e * 4 + dg) * 512:(e * 4 + dg + 1) * 512],
                                        start=False, stop=(e == 15))
                                return r
                            K.op("pe", mmv, waits=[xkey, "w_vw", bank_free[bank]],
                                 inc="cP", key=f"v_mm_{vi}")
                            K.op("act", lambda a, bank=bank, ep=ep: a.activation(
                                out=vev[ep][:], in_=pp[bank][:], func=AF.Copy),
                                waits=[f"v_mm_{vi}",
                                       f"v_spill_{vi-2}" if vi >= 2 else None],
                                inc="cA", key=f"v_ev_{vi}")
                            bank_free[bank] = f"v_ev_{vi}"
                            K.op("sync", lambda a, ktile=ktile, dg=dg, ep=ep: a.dma_start(
                                out=v_s[ktile][:, dg * 512:(dg + 1) * 512], in_=vev[ep][:]),
                                waits=[f"v_ev_{vi}"],
                                inc=f"dS{ep}", key=f"v_spill_{vi}", n=16, pre=True)
                            xst_free[par] = f"v_mm_{vi}"

            run_phase(phase1)
            p1_bar = [("dS0", em.cnt["dS0"]), ("dS1", em.cnt["dS1"])]

        if dump == "qkv":
            def fin(K):
                K.op("sync", lambda a: a.engine_nop() if hasattr(a, "engine_nop") else a.nop(),
                     waits=p1_bar)
            run_phase(fin)
            return nc, em

        with contextlib.ExitStack() as p2:
            kbuf = [p2.enter_context(nc.sbuf_tensor(f"kbuf{i}", [128, 4096], BF16)) for i in range(2)]
            vbuf = [p2.enter_context(nc.sbuf_tensor(f"vbuf{i}", [128, 4096], BF16)) for i in range(2)]
            qbuf = [p2.enter_context(nc.sbuf_tensor(f"qbuf{i}", [128, 512], BF16)) for i in range(2)]
            s_sb = [p2.enter_context(nc.sbuf_tensor(f"ssb{i}", [128, 4096], F32)) for i in range(2)]
            p_sb = [p2.enter_context(nc.sbuf_tensor(f"psb{i}", [128, 4096], BF16)) for i in range(2)]
            pn_sb = [p2.enter_context(nc.sbuf_tensor(f"pnsb{i}", [128, 4096], BF16)) for i in range(4)]
            pt = [p2.enter_context(nc.sbuf_tensor(f"pt{i}", [128, 512], BF16)) for i in range(4)]
            aev = [p2.enter_context(nc.sbuf_tensor(f"aev{i}", [128, 512], BF16)) for i in range(2)]
            mst = [p2.enter_context(nc.sbuf_tensor(f"mst{i}", [128, 32], F32)) for i in range(2)]
            nst = [p2.enter_context(nc.sbuf_tensor(f"nst{i}", [128, 4], F32)) for i in range(2)]
            zst = [p2.enter_context(nc.sbuf_tensor(f"zst{i}", [128, 4], F32)) for i in range(2)]
            zin = [p2.enter_context(nc.sbuf_tensor(f"zin{i}", [128, 4], F32)) for i in range(2)]
            dbgz_sb = p2.enter_context(nc.sbuf_tensor("dbgz_sb", [128, 64], F32)) if dump == "all" else None
            sc = [p2.enter_context(nc.psum_tensor(f"sc{i}", [128, 512], F32)) for i in range(2)]
            av = [p2.enter_context(nc.psum_tensor(f"av{i}", [128, 512], F32)) for i in range(2)]

            NU = nu
            units = [(hd, Q) for hd in range(H) for Q in range(4)]
            sci_of = {}
            _sci = 0
            for u, (hd, Q) in enumerate(units):
                sci_of[u] = _sci
                _sci += 4 * (2 * Q + 2)

            def phase2(K):
                def kg_of(u):
                    return 2 * units[u][1] + 2

                def emit_loads(u):
                    hd, Q = units[u]
                    par = u % 2
                    qfree = (f"sc_{sci_of[u-2] + 4*kg_of(u-2) - 1}" if u >= 2 else None)
                    K.op("sync", lambda a, hd=hd, Q=Q, par=par: a.dma_start(
                        out=qbuf[par][:], in_=q_s[hd][:, Q * 512:(Q + 1) * 512]),
                        waits=p1_bar + [qfree],
                        inc=f"dK{par}", key=f"qld_{u}", n=16, pre=True)
                    if Q == 0:
                        hp = hd % 2
                        kvfree = f"av_{u-5}_31" if hd >= 2 else None
                        K.op("sync", lambda a, hd=hd, hp=hp: a.dma_start(
                            out=kbuf[hp][:], in_=k_s[hd]),
                            waits=p1_bar + [kvfree],
                            inc=f"dK{2+hp}", key=f"kld_{hd}", n=16, pre=True)
                        K.op("sync", lambda a, hd=hd, hp=hp: a.dma_start(
                            out=vbuf[hp][:].rearrange("p (kt d) -> p kt d", kt=32),
                            in_=v_s[:, :, hd * 128:(hd + 1) * 128].rearrange("kt p d -> p kt d")),
                            waits=p1_bar + [kvfree],
                            inc=f"dK{4+hp}", key=f"vld_{hd}", n=16, pre=True)

                def emit_scores(u):
                    hd, Q = units[u]
                    kg = 2 * Q + 2
                    hp = hd % 2
                    par = u % 2
                    base = sci_of[u]
                    for j in range(4):
                        for gi in range(kg):
                            i = base + j * kg + gi
                            bank = i % 2
                            bw = [f"qld_{u}", f"kld_{hd}", "w_mask",
                                  f"cm_{i-2}" if i >= 2 else None]

                            def mm(a, j=j, gi=gi, bank=bank, kg=kg, hp=hp, par=par):
                                if gi >= kg - 2:
                                    gidx = (gi - (kg - 2)) * 4 + j
                                    a.matmul(sc[bank][:], g["id_b"][:],
                                             g["mask_b"][:, gidx * 512:(gidx + 1) * 512],
                                             start=True, stop=False)
                                    return a.matmul(
                                        sc[bank][:], qbuf[par][:, j * 128:(j + 1) * 128],
                                        kbuf[hp][:, gi * 512:(gi + 1) * 512],
                                        start=False, stop=True)
                                return a.matmul(
                                    sc[bank][:], qbuf[par][:, j * 128:(j + 1) * 128],
                                    kbuf[hp][:, gi * 512:(gi + 1) * 512],
                                    start=True, stop=True)
                            K.op("pe", mm, waits=bw, inc="cP", key=f"sc_{i}")

                def cm_one(u, j, kg, up):
                    slot = j % 2
                    base = sci_of[u]
                    prev = (f"exp_{u}_{j-2}" if j >= 2
                            else (f"exp_{u-1}_{j+2}" if u >= 1 else None))
                    for gi in range(kg):
                        i = base + j * kg + gi
                        bank = i % 2
                        K.op("dve", lambda a, gi=gi, bank=bank, slot=slot, up=up, j=j: a.tensor_scalar(
                            out=s_sb[slot][:, gi * 512:(gi + 1) * 512], in0=sc[bank][:],
                            scalar1=SCALE, scalar2=None, op0=ALU.mult, op1=ALU.max,
                            accum_out=mst[up][:, j * 8 + gi:j * 8 + gi + 1]),
                            waits=[f"sc_{i}", prev if gi == 0 else None],
                            inc="cV", key=f"cm_{i}")

                def red_one(u, j, kg, up):
                    last_cm = f"cm_{sci_of[u] + j * kg + kg - 1}"
                    K.op("dve", lambda a, j=j, up=up, kg=kg: a.tensor_reduce(
                        nst[up][:, j:j + 1], mst[up][:, j * 8:j * 8 + kg],
                        AX.X, ALU.max, negate=True),
                        waits=[last_cm, f"exp_{u-2}_{j}" if u >= 2 else None],
                        inc="cV", key=f"red_{u}_{j}")

                def emit_copymax(u):
                    hd, Q = units[u]
                    kg = 2 * Q + 2
                    up = u % 2
                    cm_one(u, 0, kg, up)
                    cm_one(u, 1, kg, up)
                    red_one(u, 0, kg, up)
                    red_one(u, 1, kg, up)
                    cm_one(u, 2, kg, up)
                    cm_one(u, 3, kg, up)
                    red_one(u, 2, kg, up)
                    red_one(u, 3, kg, up)

                def emit_exp(u):
                    hd, Q = units[u]
                    kg = 2 * Q + 2
                    up = u % 2
                    for j in range(4):
                        slot = j % 2
                        pfree = (f"pn_{u}_{j-2}" if j >= 2
                                 else (f"pn_{u-1}_{j+2}" if u >= 1 else None))
                        zfree = f"zinv_{u-2}_{j}" if u >= 2 else None
                        K.op("act", lambda a, j=j, slot=slot, up=up, kg=kg: a.activation(
                            out=p_sb[slot][:, 0:kg * 512], in_=s_sb[slot][:, 0:kg * 512],
                            func=AF.Exp, bias=nst[up][:, j:j + 1], scale=1.0,
                            accum_out=zst[up][:, j:j + 1]),
                            waits=[f"red_{u}_{j}", pfree, zfree],
                            inc="cA", key=f"exp_{u}_{j}")

                def emit_pnorm(u):
                    hd, Q = units[u]
                    kg = 2 * Q + 2
                    up = u % 2
                    def zinv_op(j):
                        K.op("dve", lambda a, j=j, up=up: a.reciprocal(
                            zin[up][:, j:j + 1], zst[up][:, j:j + 1]),
                            waits=[f"exp_{u}_{j}"],
                            inc="cV", key=f"zinv_{u}_{j}")

                    def pn_op(j):
                        slot = j % 2
                        K.op("dve", lambda a, j=j, slot=slot, up=up, kg=kg: a.tensor_scalar(
                            out=pn_sb[j][:, 0:kg * 512], in0=p_sb[slot][:, 0:kg * 512],
                            scalar1=zin[up][:, j:j + 1], scalar2=None, op0=ALU.mult),
                            waits=[f"zinv_{u}_{j}", f"tpe_{u-1}" if u >= 1 else None],
                            inc="cV", key=f"pn_{u}_{j}")
                    def dbg_op(j):
                        if dbg_pn is not None and u < 4:
                            K.op("sync", lambda a, u=u, j=j, kg=kg: a.dma_start(
                                out=dbg_pn[4 * u + j][:, 0:kg * 512],
                                in_=pn_sb[j][:, 0:kg * 512]),
                                waits=[f"pn_{u}_{j}"],
                                inc="dS0", key=f"dbgpn_{u}_{j}", n=16, pre=True)
                            K.op("dve", lambda a, u=u, j=j, up=up: a.tensor_copy(
                                dbgz_sb[:, (4 * u + j) * 3:(4 * u + j) * 3 + 1],
                                nst[up][:, j:j + 1]))
                            K.op("dve", lambda a, u=u, j=j, up=up: a.tensor_copy(
                                dbgz_sb[:, (4 * u + j) * 3 + 1:(4 * u + j) * 3 + 2],
                                zst[up][:, j:j + 1]))
                            K.op("dve", lambda a, u=u, j=j, up=up: a.tensor_copy(
                                dbgz_sb[:, (4 * u + j) * 3 + 2:(4 * u + j) * 3 + 3],
                                zin[up][:, j:j + 1]),
                                inc="cV", key=f"dbgz_{u}_{j}")
                    zinv_op(0)
                    zinv_op(1)
                    pn_op(0)
                    zinv_op(2)
                    pn_op(1)
                    zinv_op(3)
                    pn_op(2)
                    pn_op(3)
                    for j in range(4):
                        dbg_op(j)

                def emit_transposes(u):
                    hd, Q = units[u]
                    kg = 2 * Q + 2
                    nkt = 4 * kg
                    for kt in range(nkt):
                        ps = kt % 4
                        if kt >= 4:
                            ptfree = f"av_{u}_{kt-4}"
                        elif u >= 1:
                            nkt1 = 4 * kg_of(u - 1)
                            ptfree = f"av_{u-1}_{nkt1 - 4 + ps}"
                        else:
                            ptfree = None
                        for j in range(4):
                            K.op("sync", lambda a, j=j, kt=kt, ps=ps: a.dma_start_transpose(
                                pt[ps][:, j * 128:(j + 1) * 128],
                                pn_sb[j][:, kt * 128:(kt + 1) * 128]),
                                waits=[f"pn_{u}_3", ptfree if j == 0 else None],
                                inc=f"dT{ps}", key=f"tp_{u}_{kt}_{j}",
                                n=16, pre=(j == 0))
                    K.alias(f"tpe_{u}", f"tp_{u}_{nkt-1}_3")

                def emit_av(u):
                    hd, Q = units[u]
                    kg = 2 * Q + 2
                    nkt = 4 * kg
                    hp = hd % 2
                    bank = u % 2
                    for kt in range(nkt):
                        ps = kt % 4
                        K.op("pe", lambda a, kt=kt, ps=ps, bank=bank, hp=hp, nkt=nkt: a.matmul(
                            av[bank][:], vbuf[hp][:, kt * 128:(kt + 1) * 128], pt[ps][:],
                            start=(kt == 0), stop=(kt == nkt - 1)),
                            waits=[f"tp_{u}_{kt}_3", f"vld_{hd}",
                                   (f"aev_{u-2}" if u >= 2 else None) if kt == 0 else None],
                            inc="cP", key=f"av_{u}_{kt}")

                def emit_evac(u):
                    par = u % 2
                    hd, Q = units[u]
                    kg = 2 * Q + 2
                    K.op("act", lambda a, par=par: a.activation(
                        out=aev[par][:], in_=av[par][:], func=AF.Copy),
                        waits=[f"av_{u}_{4*kg-1}",
                               f"asp_{u-2}" if u >= 2 else None],
                        inc="cA", key=f"aev_{u}")

                def emit_spill(u):
                    hd, Q = units[u]
                    par = u % 2
                    K.op("sync", lambda a, hd=hd, Q=Q, par=par: a.dma_start(
                        out=a_s[4 * Q:4 * Q + 4, hd].rearrange("t p q -> p t q"),
                        in_=aev[par][:].rearrange("p (t q) -> p t q", t=4)),
                        waits=[f"aev_{u}"],
                        inc=f"dS{par}", key=f"asp_{u}", n=16, pre=True)

                if dbg_z is not None:
                    pass
                for it in range(NU + 2):
                    if it < NU:
                        emit_loads(it)
                        emit_scores(it)
                    if 1 <= it <= NU:
                        emit_exp(it - 1)
                        emit_pnorm(it - 1)
                    if it < NU:
                        emit_copymax(it)
                    if 1 <= it <= NU:
                        emit_transposes(it - 1)
                        emit_av(it - 1)
                    if 2 <= it:
                        emit_evac(it - 2)
                        emit_spill(it - 2)
                if dbg_z is not None:
                    K.op("sync", lambda a: a.dma_start(out=dbg_z[:], in_=dbgz_sb[:]),
                         waits=[f"dbgz_3_3"], inc="dS0", key="dbgz_spill", n=16, pre=True)

            run_phase(phase2)
            p2_bar = [("dS0", em.cnt["dS0"]), ("dS1", em.cnt["dS1"])]

        if dump == "attn":
            def fin2(K):
                K.op("sync", lambda a: a.nop(), waits=p2_bar)
            run_phase(fin2)
            return nc, em

        p12.close()

        with contextlib.ExitStack() as p34:
            kq = p34.enter_context(nc.sbuf_tensor("kq", [128, 16 * 2048], BF16))
            skb = p34.enter_context(nc.sbuf_tensor("skb", [128, 2048], F32))
            bqb = p34.enter_context(nc.sbuf_tensor("bqb", [128, 2048], F32))

            with contextlib.ExitStack() as p35:
                fwb = [p35.enter_context(nc.sbuf_tensor(f"fwb{i}", [128, 2048], F32)) for i in range(2)]
                amx = p35.enter_context(nc.sbuf_tensor("amx", [128, 16], F32))
                amxT = p35.enter_context(nc.sbuf_tensor("amxT", [16, 128], F32))
                arow = p35.enter_context(nc.sbuf_tensor("arow", [1, 2048], F32))
                rrow = p35.enter_context(nc.sbuf_tensor("rrow", [1, 2048], F32))
                srow = p35.enter_context(nc.sbuf_tensor("srow", [1, 2048], F32))
                rb = p35.enter_context(nc.sbuf_tensor("rb", [128, 2048], F32))
                fbr = p35.enter_context(nc.sbuf_tensor("fbr", [1, 2048], F32))
                fbq = p35.enter_context(nc.sbuf_tensor("fbq", [1, 2048], F32))
                fbs = p35.enter_context(nc.sbuf_tensor("fbs", [1, 2], F32))
                scr = p35.enter_context(nc.sbuf_tensor("scr", [128, 2048], F32))
                pq = [p35.enter_context(nc.psum_tensor(f"pq{i}", [128, 512], F32)) for i in range(2)]

                def phase35(K):
                    for t in range(16):
                        par = t % 2
                        K.op("sync", lambda a, t=t, par=par: a.dma_start(
                            out=fwb[par][:], in_=fwt_st[t]),
                            waits=p2_bar + [f"famx_{t-2}" if t >= 2 else None],
                            inc=f"dK{par}", key=f"fwtl_{t}", n=16, pre=True)
                        K.op("dve", lambda a, t=t, par=par: a.tensor_reduce(
                            amx[:, t:t + 1], fwb[par][:], AX.X, ALU.max,
                            apply_absolute_value=True),
                            waits=[f"fwtl_{t}"], inc="cV", key=f"famx_{t}")
                    K.op("pe", lambda a: a.transpose(pq[0][0:16, 0:128], amx[:], g["id_f"][:]),
                         waits=["famx_15", "w_idf"], inc="cP", key="famxT")
                    K.op("dve", lambda a: a.tensor_copy(amxT[:], pq[0][0:16, 0:128]),
                         waits=["famxT"], inc="cV", key="famxT_ev")
                    for c in range(16):
                        K.op("sync", lambda a, c=c: a.dma_start(
                            out=arow[0:1, c * 128:(c + 1) * 128], in_=amxT[c:c + 1, :]),
                            waits=["famxT_ev"],
                            inc=f"dS{c % 2}", key=f"arow_{c}", n=16, pre=True)
                    K.op("dve", lambda a: a.tensor_scalar(
                        out=arow[:], in0=arow[:], scalar1=TINY, scalar2=None, op0=ALU.max),
                        waits=["arow_14", "arow_15"], inc="cV", key="arow_c")
                    K.op("dve", lambda a: a.tensor_scalar(
                        out=srow[:], in0=arow[:], scalar1=1.0 / QMAX, scalar2=None, op0=ALU.mult),
                        waits=["arow_c"], inc="cV", key="srow_ok")
                    K.op("dve", lambda a: a.reciprocal(rrow[:], srow[:]),
                        waits=["srow_ok"], inc="cV", key="rrow_ok")
                    for seg in range(4):
                        K.op("pe", lambda a, seg=seg: a.matmul(
                            pq[0][:], g["ones_f"][:], rrow[0:1, seg * 512:(seg + 1) * 512],
                            start=True, stop=True),
                            waits=["rrow_ok", "ones_ready",
                                   f"rb_ev_{seg-1}" if seg else "famxT_ev"],
                            inc="cP", key=f"rb_mm_{seg}")
                        K.op("dve", lambda a, seg=seg: a.tensor_copy(
                            rb[:, seg * 512:(seg + 1) * 512], pq[0][:]),
                            waits=[f"rb_mm_{seg}"], inc="cV", key=f"rb_ev_{seg}")
                        K.op("pe", lambda a, seg=seg: a.matmul(
                            pq[1][:], g["ones_f"][:], srow[0:1, seg * 512:(seg + 1) * 512],
                            start=True, stop=True),
                            waits=["srow_ok", f"sk_ev_{seg-1}" if seg else None],
                            inc="cP", key=f"sk_mm_{seg}")
                        K.op("dve", lambda a, seg=seg: a.tensor_copy(
                            skb[:, seg * 512:(seg + 1) * 512], pq[1][:]),
                            waits=[f"sk_mm_{seg}"], inc="cV", key=f"sk_ev_{seg}")
                    K.op("gp", lambda a: a.dma_start(out=fbr[:], in_=fb_r[:]),
                         waits=p2_bar, inc="dW", key="fbr_ld", n=16, pre=True)
                    K.op("dve", lambda a: a.tensor_reduce(
                        fbs[0:1, 0:1], fbr[:], AX.X, ALU.max, apply_absolute_value=True),
                        waits=["fbr_ld"], inc="cV", key="fba_0")
                    K.op("dve", lambda a: a.tensor_scalar(
                        out=fbs[0:1, 0:1], in0=fbs[0:1, 0:1], scalar1=TINY, scalar2=None,
                        op0=ALU.max), waits=["fba_0"], inc="cV", key="fba_1")
                    K.op("dve", lambda a: a.tensor_scalar(
                        out=fbs[0:1, 0:1], in0=fbs[0:1, 0:1], scalar1=1.0 / QMAX,
                        scalar2=None, op0=ALU.mult),
                        waits=["fba_1"], inc="cV", key="fba_2")
                    K.op("dve", lambda a: a.reciprocal(fbs[0:1, 1:2], fbs[0:1, 0:1]),
                        waits=["fba_2"], inc="cV", key="fbs_ok")
                    K.op("dve", lambda a: a.tensor_scalar(
                        out=fbq[:], in0=fbr[:], scalar1=fbs[0:1, 1:2], scalar2=MAGIC,
                        op0=ALU.mult, op1=ALU.add),
                        waits=["fbs_ok"])
                    K.op("dve", lambda a: a.tensor_scalar(
                        out=fbq[:], in0=fbq[:], scalar1=MAGIC, scalar2=QMAX,
                        op0=ALU.subtract, op1=ALU.min))
                    K.op("dve", lambda a: a.tensor_scalar(
                        out=fbq[:], in0=fbq[:], scalar1=-QMAX, scalar2=fbs[0:1, 0:1],
                        op0=ALU.max, op1=ALU.mult),
                        waits=["fba_2"], inc="cV", key="fbq_ok")
                    for seg in range(4):
                        K.op("pe", lambda a, seg=seg: a.matmul(
                            pq[0][:], g["ones_f"][:], fbq[0:1, seg * 512:(seg + 1) * 512],
                            start=True, stop=True),
                            waits=["fbq_ok", "rb_ev_3",
                                   f"bq_ev_{seg-1}" if seg else None],
                            inc="cP", key=f"bq_mm_{seg}")
                        K.op("dve", lambda a, seg=seg: a.tensor_copy(
                            bqb[:, seg * 512:(seg + 1) * 512], pq[0][:]),
                            waits=[f"bq_mm_{seg}"], inc="cV", key=f"bq_ev_{seg}")
                    if dbg_fb is not None:
                        K.op("sync", lambda a: a.dma_start(out=dbg_fb[0], in_=bqb[:]),
                             waits=["bq_ev_3"], inc="dS0", key="dbgfb0", n=16, pre=True)
                        K.op("sync", lambda a: a.dma_start(out=dbg_fb[1][0:1, :], in_=fbq[:]),
                             waits=["fbq_ok"], inc="dS0", key="dbgfb1", n=16, pre=True)
                        K.op("sync", lambda a: a.dma_start(out=dbg_fb[2][0:1, :], in_=fbr[:]),
                             waits=["fbq_ok"], inc="dS0", key="dbgfb2", n=16, pre=True)
                    for dc in range(16):
                        par = dc % 2
                        K.op("sync", lambda a, dc=dc, par=par: a.dma_start(
                            out=fwb[par][:], in_=fw_st[dc * 128:(dc + 1) * 128, :]),
                            waits=["famx_15", f"kq_{dc-2}" if dc >= 2 else None],
                            inc=f"dK{par}", key=f"fwl_{dc}", n=16, pre=True)
                        K.op("dve", lambda a, par=par: a.tensor_tensor(
                            scr[:], fwb[par][:], rb[:], ALU.mult),
                            waits=[f"fwl_{dc}", "rb_ev_3"])
                        K.op("dve", lambda a: a.tensor_scalar(
                            out=scr[:], in0=scr[:], scalar1=MAGIC, scalar2=MAGIC,
                            op0=ALU.add, op1=ALU.subtract))
                        K.op("dve", lambda a, dc=dc: a.tensor_scalar(
                            out=kq[:, dc * 2048:(dc + 1) * 2048], in0=scr[:],
                            scalar1=QMAX, scalar2=-QMAX, op0=ALU.min, op1=ALU.max),
                            inc="cV", key=f"kq_{dc}")
                    if dbg_kq is not None:
                        K.op("sync", lambda a: a.dma_start(out=dbg_kq[:], in_=kq[:]),
                             waits=["kq_15"], inc="dS0", key="dbgkq", n=16, pre=True)

                run_phase(phase35)

            with contextlib.ExitStack() as p3:
                wo_b = p3.enter_context(nc.sbuf_tensor("wo_b", [128, 32768], BF16))
                atile = [p3.enter_context(nc.sbuf_tensor(f"atl{i}", [128, 2048], BF16)) for i in range(2)]
                xtm = p3.enter_context(nc.sbuf_tensor("xtm", [128, 2048], F32))
                bo_b = p3.enter_context(nc.sbuf_tensor("bo_b", [1, 2048], BF16))
                bufA = p3.enter_context(nc.sbuf_tensor("bufA", [128, 2048], F32))
                bufB = p3.enter_context(nc.sbuf_tensor("bufB", [128, 2048], F32))
                bufC = p3.enter_context(nc.sbuf_tensor("bufC", [128, 2048], F32))
                xqtm = p3.enter_context(nc.sbuf_tensor("xqtm", [128, 2048], BF16))
                xqfm = p3.enter_context(nc.sbuf_tensor("xqfm", [128, 2048], BF16))
                st3 = p3.enter_context(nc.sbuf_tensor("st3", [128, 10], F32))
                po = [p3.enter_context(nc.psum_tensor(f"po{i}", [128, 512], F32)) for i in range(4)]

                def phase3(K):
                    p35_bar = ["kq_15", "bq_mm_3", "sk_ev_3"]
                    K.op("gp", lambda a: a.dma_start(
                        out=wo_b[:].rearrange("p (h e f) -> p h e f", h=16, e=4),
                        in_=wo_st[:]), waits=p2_bar + p35_bar,
                        inc="dW", key="w_wo", n=16, pre=True)
                    K.op("gp", lambda a: a.dma_start(out=bo_b[:], in_=bo_r[:]),
                         waits=p2_bar + p35_bar, inc="dW", key="w_bo", n=16, pre=True)
                    for t in range(16):
                        par = t % 2
                        K.op("sync", lambda a, t=t, par=par: a.dma_start(
                            out=atile[par][:].rearrange("p (h q) -> p h q", h=16),
                            in_=a_s[t].rearrange("h p q -> p h q")),
                            waits=p2_bar + p35_bar + [f"op_mm_{t-2}_3" if t >= 2 else None],
                            inc=f"dK{par}", key=f"atl_{t}", n=16, pre=True)
                        K.op("sync", lambda a, t=t: a.dma_start(
                            out=xtm[:], in_=xtm_st[t]),
                            waits=p2_bar + p35_bar + [f"r1_{t-1}_3" if t >= 1 else None],
                            inc=f"dK{2 + par}", key=f"xtm_{t}", n=16, pre=True)
                        if dbg_at is not None:
                            K.op("sync", lambda a, t=t, par=par: a.dma_start(
                                out=dbg_at[t], in_=atile[par][:]),
                                waits=[f"atl_{t}"],
                                inc="dS0", key=f"dbgat_{t}", n=16, pre=True)
                        for eg in range(4):
                            def mmo(a, t=t, eg=eg, par=par):
                                a.matmul(po[eg][:], g["ones_b"][0:1, 0:128],
                                         bo_b[0:1, eg * 512:(eg + 1) * 512],
                                         start=True, stop=False)
                                r = None
                                for hd in range(16):
                                    r = a.matmul(
                                        po[eg][:], atile[par][:, hd * 128:(hd + 1) * 128],
                                        wo_b[:, (hd * 4 + eg) * 512:(hd * 4 + eg + 1) * 512],
                                        start=False, stop=(hd == 15))
                                return r
                            K.op("pe", mmo,
                                 waits=[f"atl_{t}", "w_wo", "w_bo",
                                        f"r1_{t-1}_{eg}" if t >= 1 else None],
                                 inc="cP", key=f"op_mm_{t}_{eg}")
                        for eg in range(4):
                            K.op("dve", lambda a, eg=eg: a.tensor_tensor(
                                bufA[:, eg * 512:(eg + 1) * 512], po[eg][:],
                                xtm[:, eg * 512:(eg + 1) * 512], ALU.add),
                                waits=[f"op_mm_{t}_{eg}", f"xtm_{t}",
                                       f"ln1sp_{t-1}" if (eg == 0 and t >= 1) else None,
                                       f"xq_{t-1}" if (eg == 0 and t >= 1) else None],
                                inc="cV", key=f"r1_{t}_{eg}")
                        if dbg_r1 is not None:
                            K.op("sync", lambda a, t=t: a.dma_start(
                                out=dbg_r1[t], in_=bufA[:]),
                                waits=[f"r1_{t}_3"],
                                inc="dS1", key=f"dbgr1_{t}", n=16, pre=True)
                        K.op("dve", lambda a: a.tensor_reduce(
                            st3[:, 0:1], bufA[:], AX.X, ALU.add),
                            waits=[f"dbgst_{t-1}"] if (dbg_st is not None and t >= 1) else [],
                            inc="cV", key=f"sum_{t}")
                        K.op("dve", lambda a: a.tensor_scalar(
                            out=st3[:, 1:2], in0=st3[:, 0:1], scalar1=1.0 / 2048.0,
                            scalar2=None, op0=ALU.mult),
                            waits=[f"sum_{t}"], inc="cV", key=f"mean_{t}")
                        K.op("dve", lambda a: a.tensor_scalar(
                            out=bufB[:], in0=bufA[:], scalar1=st3[:, 1:2],
                            scalar2=None, op0=ALU.subtract),
                            waits=[f"mean_{t}"],
                            inc="cV", key=f"cent_{t}")
                        K.op("act", lambda a: a.activation(
                            out=bufC[:], in_=bufB[:], func=AF.Square,
                            accum_out=st3[:, 2:3]),
                            waits=[f"cent_{t}"], inc="cA", key=f"sq_{t}")
                        K.op("dve", lambda a: a.tensor_scalar(
                            out=st3[:, 3:4], in0=st3[:, 2:3], scalar1=1.0 / 2048.0,
                            scalar2=EPS, op0=ALU.mult, op1=ALU.add),
                            waits=[f"sq_{t}"], inc="cV", key=f"var_{t}")
                        K.op("act", lambda a: a.activation(
                            out=st3[:, 4:5], in_=st3[:, 3:4], func=AF.Sqrt),
                            waits=[f"var_{t}"], inc="cA", key=f"sd_{t}")
                        K.op("dve", lambda a: a.reciprocal(st3[:, 5:6], st3[:, 4:5]),
                             waits=[f"sd_{t}"], inc="cV", key=f"rstd_{t}")
                        K.op("dve", lambda a: a.tensor_scalar(
                            out=bufA[:], in0=bufB[:], scalar1=st3[:, 5:6],
                            scalar2=None, op0=ALU.mult),
                            waits=[f"rstd_{t}"] + ([f"dbgr1_{t}"] if dbg_r1 is not None else []),
                            inc="cV", key=f"ln1_{t}")
                        K.op("sync", lambda a, t=t: a.dma_start(
                            out=ln1_s[t], in_=bufA[:]),
                            waits=[f"ln1_{t}"],
                            inc=f"dS{par}", key=f"ln1sp_{t}", n=16, pre=True)
                        if dbg_st is not None:
                            K.op("sync", lambda a, t=t: a.dma_start(
                                out=dbg_st[t], in_=st3[:]),
                                waits=[f"xq_{t}"],
                                inc="dS0", key=f"dbgst_{t}", n=16, pre=True)
                        K.op("dve", lambda a: a.tensor_reduce(
                            st3[:, 6:7], bufA[:], AX.X, ALU.max, apply_absolute_value=True),
                            inc="cV", key=f"amx_{t}")
                        K.op("dve", lambda a: a.tensor_scalar(
                            out=st3[:, 6:7], in0=st3[:, 6:7], scalar1=TINY, scalar2=None,
                            op0=ALU.max), waits=[f"amx_{t}"], inc="cV", key=f"amxc_{t}")
                        K.op("dve", lambda a, t=t: a.tensor_scalar(
                            out=g["sx_all"][:, t:t + 1], in0=st3[:, 6:7],
                            scalar1=1.0 / QMAX, scalar2=None, op0=ALU.mult),
                            waits=[f"amxc_{t}"], inc="cV", key=f"sx_{t}")
                        K.op("dve", lambda a, t=t: a.reciprocal(
                            st3[:, 7:8], g["sx_all"][:, t:t + 1]),
                            waits=[f"sx_{t}"], inc="cV", key=f"rx_{t}")
                        K.op("dve", lambda a: a.tensor_scalar(
                            out=bufB[:], in0=bufA[:], scalar1=st3[:, 7:8],
                            scalar2=MAGIC, op0=ALU.mult, op1=ALU.add),
                            waits=[f"rx_{t}"])
                        K.op("dve", lambda a: a.tensor_scalar(
                            out=bufC[:], in0=bufB[:], scalar1=MAGIC, scalar2=QMAX,
                            op0=ALU.subtract, op1=ALU.min),
                            waits=[f"sq_{t}"])
                        K.op("dve", lambda a, t=t: a.tensor_scalar(
                            out=xqtm[:], in0=bufC[:], scalar1=-QMAX, scalar2=None,
                            op0=ALU.max),
                            waits=[f"xqt_{t-1}_15" if t >= 1 else None,
                                   f"dbgxt_{t-1}" if (dbg_xt is not None and t >= 1) else None],
                            inc="cV", key=f"xq_{t}")
                        if dbg_xt is not None:
                            K.op("sync", lambda a, t=t: a.dma_start(
                                out=dbg_xt[t], in_=xqtm[:]),
                                waits=[f"xq_{t}"], inc="dS0", key=f"dbgxt_{t}", n=16, pre=True)
                        for dc in range(16):
                            K.op("sync", lambda a, dc=dc: a.dma_start_transpose(
                                xqfm[:, dc * 128:(dc + 1) * 128],
                                xqtm[:, dc * 128:(dc + 1) * 128]),
                                waits=[f"xq_{t}",
                                       f"xqsp_{t-1}" if (dc == 0 and t >= 1) else None],
                                inc="dT0", key=f"xqt_{t}_{dc}", n=16, pre=(dc == 0))
                        K.op("sync", lambda a, t=t: a.dma_start(
                            out=xq_s[t], in_=xqfm[:]),
                            waits=[f"xqt_{t}_15"],
                            inc=f"dS{par}", key=f"xqsp_{t}", n=16, pre=True)

                run_phase(phase3)
                p3_bar = [("dS0", em.cnt["dS0"]), ("dS1", em.cnt["dS1"])]

            with contextlib.ExitStack() as p4:
                ln1b = [p4.enter_context(nc.sbuf_tensor(f"ln1b{i}", [128, 2048], F32)) for i in range(2)]
                xqf = [p4.enter_context(nc.sbuf_tensor(f"xqf{i}", [128, 2048], BF16)) for i in range(2)]
                bufA = p4.enter_context(nc.sbuf_tensor("bufA4", [128, 2048], F32))
                bufB = p4.enter_context(nc.sbuf_tensor("bufB4", [128, 2048], F32))
                bufC = p4.enter_context(nc.sbuf_tensor("bufC4", [128, 2048], F32))
                outb = [p4.enter_context(nc.sbuf_tensor(f"outb{i}", [128, 2048], F32)) for i in range(2)]
                st4 = p4.enter_context(nc.sbuf_tensor("st4", [128, 10], F32))
                pf = [p4.enter_context(nc.psum_tensor(f"pf{i}", [128, 512], F32)) for i in range(4)]

                def phase4(K):
                    for t in range(16):
                        par = t % 2
                        K.op("sync", lambda a, t=t, par=par: a.dma_start(
                            out=xqf[par][:], in_=xq_s[t]),
                            waits=p3_bar + [f"ffn_mm_{t-2}_3" if t >= 2 else None],
                            inc=f"dK{par}", key=f"xql_{t}", n=16, pre=True)
                        K.op("sync", lambda a, t=t, par=par: a.dma_start(
                            out=ln1b[par][:], in_=ln1_s[t]),
                            waits=p3_bar + [f"r2_{t-2}" if t >= 2 else None],
                            inc=f"dK{2 + par}", key=f"ln1l_{t}", n=16, pre=True)
                        for fg in range(4):
                            def mmf(a, t=t, fg=fg, par=par):
                                r = None
                                for dc in range(16):
                                    r = a.matmul(
                                        pf[fg][:], xqf[par][:, dc * 128:(dc + 1) * 128],
                                        kq[:, dc * 2048 + fg * 512: dc * 2048 + (fg + 1) * 512],
                                        start=(dc == 0), stop=(dc == 15))
                                return r
                            K.op("pe", mmf,
                                 waits=[f"xql_{t}", "kq_15",
                                        f"ys_{t-1}_{fg}" if t >= 1 else None],
                                 inc="cP", key=f"ffn_mm_{t}_{fg}")
                        for fg in range(4):
                            K.op("dve", lambda a, fg=fg, t=t: a.scalar_tensor_tensor(
                                out=bufC[:, fg * 512:(fg + 1) * 512], in0=pf[fg][:],
                                scalar=g["sx_all"][:, t:t + 1],
                                in1=skb[:, fg * 512:(fg + 1) * 512],
                                op0=ALU.mult, op1=ALU.mult),
                                waits=[f"ffn_mm_{t}_{fg}", f"sx_{t}", "sk_ev_3",
                                       f"cent2_{t-1}" if (fg == 0 and t >= 1) else None,
                                       (f"dbgr2_{t-1}" if (dbg_r2 is not None and fg == 0 and t >= 1) else None)],
                                inc="cV", key=f"ys_{t}_{fg}")
                        K.op("dve", lambda a: a.tensor_tensor(
                            bufA[:], bufC[:], bqb[:], ALU.add), waits=["bq_ev_3"],
                            inc="cV", key=f"yb_{t}")
                        if dbg_ys is not None:
                            K.op("sync", lambda a, t=t: a.dma_start(
                                out=dbg_ys[t], in_=bufA[:]),
                                waits=[f"yb_{t}"], inc="dS0", key=f"dbgys_{t}", n=16, pre=True)
                        K.op("dve", lambda a: a.tensor_scalar(
                            out=bufC[:], in0=bufA[:], scalar1=0.0, scalar2=None,
                            op0=ALU.max, op1=ALU.max, accum_out=st4[:, 0:1]),
                            waits=([f"dbgys_{t}"] if dbg_ys is not None else []) +
                                  ([f"dbgs4_{t-1}"] if (dbg_s4 is not None and t >= 1) else []),
                            inc="cV", key=f"relu_{t}")
                        if dbg_yr is not None:
                            K.op("sync", lambda a, t=t: a.dma_start(
                                out=dbg_yr[t], in_=bufC[:]),
                                waits=[f"relu_{t}"], inc="dS1", key=f"dbgyr_{t}", n=16, pre=True)
                        K.op("dve", lambda a: a.tensor_scalar(
                            out=st4[:, 1:2], in0=st4[:, 0:1], scalar1=TINY, scalar2=None,
                            op0=ALU.max), waits=[f"relu_{t}"], inc="cV", key=f"ayc_{t}")
                        K.op("dve", lambda a: a.tensor_scalar(
                            out=st4[:, 2:3], in0=st4[:, 1:2], scalar1=1.0 / QMAX,
                            scalar2=None, op0=ALU.mult),
                            waits=[f"ayc_{t}"], inc="cV", key=f"sy_{t}")
                        K.op("dve", lambda a: a.reciprocal(st4[:, 3:4], st4[:, 2:3]),
                            waits=[f"sy_{t}"], inc="cV", key=f"ry_{t}")
                        K.op("dve", lambda a: a.tensor_scalar(
                            out=bufA[:], in0=bufC[:], scalar1=st4[:, 3:4],
                            scalar2=MAGIC, op0=ALU.mult, op1=ALU.add),
                            waits=[f"ry_{t}"])
                        K.op("dve", lambda a: a.tensor_scalar(
                            out=bufC[:], in0=bufA[:], scalar1=MAGIC, scalar2=QMAX,
                            op0=ALU.subtract, op1=ALU.min),
                            waits=[f"dbgyr_{t}"] if dbg_yr is not None else [])
                        K.op("dve", lambda a: a.tensor_scalar(
                            out=bufA[:], in0=bufC[:], scalar1=-QMAX, scalar2=None,
                            op0=ALU.max))
                        K.op("dve", lambda a, t=t, par=par: a.scalar_tensor_tensor(
                            out=bufC[:], in0=bufA[:], scalar=st4[:, 2:3],
                            in1=ln1b[par][:], op0=ALU.mult, op1=ALU.add),
                            waits=[f"ln1l_{t}", f"sy_{t}"], inc="cV", key=f"r2_{t}")
                        if dbg_r2 is not None:
                            K.op("sync", lambda a, t=t: a.dma_start(
                                out=dbg_r2[t], in_=bufC[:]),
                                waits=[f"r2_{t}"], inc="dS0", key=f"dbgr2_{t}", n=16, pre=True)
                            K.op("sync", lambda a, t=t: a.dma_start(
                                out=dbg_s4[t], in_=st4[:]),
                                waits=[f"r2_{t}"], inc="dS0", key=f"dbgs4_{t}", n=16, pre=True)
                        K.op("dve", lambda a: a.tensor_reduce(
                            st4[:, 4:5], bufC[:], AX.X, ALU.add),
                            inc="cV", key=f"sum2_{t}")
                        K.op("dve", lambda a: a.tensor_scalar(
                            out=st4[:, 4:5], in0=st4[:, 4:5], scalar1=1.0 / 2048.0,
                            scalar2=None, op0=ALU.mult),
                            waits=[f"sum2_{t}"], inc="cV", key=f"mean2_{t}")
                        K.op("dve", lambda a: a.tensor_scalar(
                            out=bufA[:], in0=bufC[:], scalar1=st4[:, 4:5],
                            scalar2=None, op0=ALU.subtract),
                            waits=[f"mean2_{t}", f"sq2_{t-1}" if t >= 1 else None],
                            inc="cV", key=f"cent2_{t}")
                        K.op("act", lambda a: a.activation(
                            out=bufB[:], in_=bufA[:], func=AF.Square,
                            accum_out=st4[:, 5:6]),
                            waits=[f"cent2_{t}"], inc="cA", key=f"sq2_{t}")
                        K.op("dve", lambda a: a.tensor_scalar(
                            out=st4[:, 6:7], in0=st4[:, 5:6], scalar1=1.0 / 2048.0,
                            scalar2=EPS, op0=ALU.mult, op1=ALU.add),
                            waits=[f"sq2_{t}"], inc="cV", key=f"ve2_{t}")
                        K.op("act", lambda a: a.activation(
                            out=st4[:, 6:7], in_=st4[:, 6:7], func=AF.Sqrt),
                            waits=[f"ve2_{t}"], inc="cA", key=f"sd2_{t}")
                        K.op("dve", lambda a: a.reciprocal(st4[:, 5:6], st4[:, 6:7]),
                             waits=[f"sd2_{t}"], inc="cV", key=f"rstd2_{t}")
                        K.op("dve", lambda a, par=par: a.tensor_scalar(
                            out=outb[par][:], in0=bufA[:], scalar1=st4[:, 5:6],
                            scalar2=None, op0=ALU.mult),
                            waits=[f"rstd2_{t}", f"out_{t-2}" if t >= 2 else None],
                            inc="cV", key=f"ln2_{t}")
                        K.op("sync", lambda a, t=t, par=par: a.dma_start(
                            out=y_out[t], in_=outb[par][:]),
                            waits=[f"ln2_{t}"],
                            inc=f"dO{par}", key=f"out_{t}", n=16, pre=True)
                    K.op("sync", lambda a: a.nop(),
                         waits=[("dO0", 8 * 16), ("dO1", 8 * 16)])

                run_phase(phase4)

    return nc, em



def rope_tables():
    pos = np.arange(S, dtype=np.float64)[:, None]
    exps = -np.arange(0, DH, 2, dtype=np.float64) / DH
    thetas = np.repeat(10000.0 ** exps, 2)
    ang = pos * thetas[None, :]
    cos = np.ascontiguousarray(np.cos(ang).T.astype(np.float32))
    sin = np.sin(ang).T.astype(np.float32)
    sign = np.where(np.arange(DH) % 2 == 0, -1.0, 1.0).astype(np.float32)
    sin_signed = np.ascontiguousarray(sin * sign[:, None])
    return cos, sin_signed


def build_masks(h):
    m = np.zeros((128, 8, 512), np.float32)
    p = np.arange(128)[:, None]
    f = np.arange(512)[None, :]
    for j in range(4):
        Mj = np.where(f <= j * 128 + p, 0.0, NEGBIG).astype(np.float32)
        if h == 1:
            m[:, 0 + j] = 0.0
            m[:, 4 + j] = Mj
        else:
            m[:, 0 + j] = Mj
            m[:, 4 + j] = NEGBIG
    return m


def stage_inputs(inputs, core):
    b, h = core // 2, core % 2
    x = np.asarray(inputs["L0_attn_x"])[b]
    wq = np.asarray(inputs["L0_attn_Wq"]); wk = np.asarray(inputs["L0_attn_Wk"])
    wv = np.asarray(inputs["L0_attn_Wv"]); wo = np.asarray(inputs["L0_attn_Wo"])
    bq = np.asarray(inputs["L0_attn_bq"]); bk = np.asarray(inputs["L0_attn_bk"])
    bv = np.asarray(inputs["L0_attn_bv"]); bo = np.asarray(inputs["L0_attn_bo"])
    fw = np.asarray(inputs["L0_ffn_kernel"]); fb = np.asarray(inputs["L0_ffn_bias"])

    xT = np.ascontiguousarray(x.T)
    tok_idx = np.concatenate([np.arange((2 * Q + h) * 512, (2 * Q + h + 1) * 512)
                              for Q in range(4)])
    cos, sin = rope_tables()

    return {
        "xT_st": np.ascontiguousarray(xT.reshape(16, 128, 8, 512).transpose(2, 1, 0, 3)),
        "xTq_st": np.ascontiguousarray(xT[:, tok_idx].reshape(16, 128, 4, 512).transpose(2, 1, 0, 3)),
        "xtm_st": np.ascontiguousarray(x[tok_idx].reshape(16, 128, 2048)),
        "wq_st": np.ascontiguousarray(wq.reshape(16, 128, 16, 128).transpose(1, 2, 0, 3)),
        "wk_st": np.ascontiguousarray(wk.reshape(16, 128, 16, 128).transpose(1, 2, 0, 3)),
        "wv_st": np.ascontiguousarray(wv.reshape(16, 128, 4, 512).transpose(1, 0, 2, 3)),
        "wo_st": np.ascontiguousarray(wo.reshape(16, 128, 4, 512).transpose(1, 0, 2, 3)),
        "fw_st": np.ascontiguousarray(fw),
        "fwt_st": np.ascontiguousarray(fw.T.reshape(16, 128, 2048)),
        "bq_r": np.ascontiguousarray(bq[None, :]),
        "bk_r": np.ascontiguousarray(bk[None, :]),
        "bv_r": np.ascontiguousarray(bv[None, :]),
        "fb_r": np.ascontiguousarray(fb[None, :]),
        "bo_r": np.ascontiguousarray(bo[None, :]),
        "cos_st": cos, "sin_st": sin,
        "cosq_st": np.ascontiguousarray(cos[:, tok_idx]),
        "sinq_st": np.ascontiguousarray(sin[:, tok_idx]),
        "mask_st": build_masks(h),
        "id_st": np.eye(128, dtype=np.float32),
    }


_CACHE = {}


def get_nc(dump=None):
    key = ("nc", dump)
    if key not in _CACHE:
        _CACHE[key] = build(dump=dump)[0]
    return _CACHE[key]


def kernel(**inputs):
    nc = get_nc()
    in_maps = [stage_inputs(inputs, c) for c in range(8)]
    res = run_bass_kernel_spmd(nc, in_maps, core_ids=list(range(8)))
    out = np.zeros((B, S, E), np.float32)
    for c in range(8):
        b, h = c // 2, c % 2
        y = res.results[c]["y_out"]
        for Q in range(4):
            for j in range(4):
                r0 = (2 * Q + h) * 512 + j * 128
                out[b, r0:r0 + 128] = y[4 * Q + j]
    return out
